# version 15
# baseline (speedup 1.0000x reference)
import numpy as np
import concourse.bass as bass
import concourse.mybir as mybir
from concourse.bass_utils import run_bass_kernel_spmd

F32 = mybir.dt.float32
F32R = mybir.dt.float32r
BF16 = mybir.dt.bfloat16
AF = mybir.ActivationFunctionType
ALU = mybir.AluOpType

B, L, DM = 4, 4096, 2048
DH, NJ, NN, RK, FF = 1024, 8, 16, 128, 8192
EPS_LN, EPS_Q = 1e-5, 1e-8

NLT = L // 128
TSC = 256
NCH = L // TSC
TAU = 512
NTAU = L // TAU
NQH = 4
NTQ = 2
NFC = FF // 128
NDK = DM // 128
NDM = DH // 128
NFK = FF // 128
NGR = NQH * NFC * NTQ
ZRING = 5
FKG = 8
NZG = NFK // FKG
W2T_TILES = FF // 128
NQT = NFC + NFK

STAGE = "full"
DBG = False

MM1_START_CH = 9
ABS_SCHED = {0: 32, 1: 32, 2: 32, 3: 32}
TERN_SCHED = {6: 22, 7: 21, 8: 21, 9: 10, 10: 9, 11: 9, 12: 9, 13: 9, 14: 9, 15: 9}


def _zev_sched():
    sched = [0] * NCH
    per = -(-NGR // (NCH - MM1_START_CH))
    left = NGR
    for c in range(MM1_START_CH, NCH):
        take = min(per, left)
        sched[c] = take
        left -= take
    assert left == 0
    return sched


def build():
    nc = bass.Bass("TRN2", target_bir_lowering=False)
    want_ssm = STAGE in ("ssm", "full")
    want_ffn = STAGE == "full"
    zev = _zev_sched() if want_ffn else [0] * NCH

    def kind_dbg(stage_match):
        return "ExternalOutput" if (DBG and STAGE == stage_match) else "Internal"

    x_tm = nc.dram_tensor("x_tm", [L, DM], F32, kind="ExternalInput")
    x_dmj = nc.dram_tensor("x_dmj", [NDM * NTAU * 128, TAU], F32, kind="ExternalInput")
    A_in = nc.dram_tensor("A_in", [128, NJ * NN], F32, kind="ExternalInput")
    D_in = nc.dram_tensor("D_in", [128, NJ], F32, kind="ExternalInput")
    dpb_in = nc.dram_tensor("dpb_in", [128, NJ], F32, kind="ExternalInput")
    dpT_in = nc.dram_tensor("dpT_in", [RK, DH], F32, kind="ExternalInput")
    xpT_in = nc.dram_tensor("xpT_in", [128, NDK * 160], F32, kind="ExternalInput")
    pat_in = nc.dram_tensor("pat_in", [NN, NN * 128], F32R, kind="ExternalInput")
    onesr_in = nc.dram_tensor("onesr_in", [1, 128], F32, kind="ExternalInput")
    ones_in = nc.dram_tensor("ones_in", [128, 1], F32, kind="ExternalInput")
    w1t_in = nc.dram_tensor("w1t_in", [NFC * 128, 2048], F32, kind="ExternalInput")
    w2T_in = nc.dram_tensor("w2T_in", [FF, DM], F32, kind="ExternalInput")
    w2Th_in = nc.dram_tensor("w2Th_in", [FF, DH], F32, kind="ExternalInput")
    b1_in = nc.dram_tensor("b1_in", [128, NFC], F32, kind="ExternalInput")
    b2h_in = nc.dram_tensor("b2h_in", [128, NDM], F32, kind="ExternalInput")

    out_dm = nc.dram_tensor("out_dm", [NDM * NTAU * 128, TAU], F32, kind="ExternalOutput")
    xn_d = nc.dram_tensor("xn_d", [L, DM], BF16, kind=kind_dbg("ln"))
    s_d = nc.dram_tensor("s_d", [NJ * NCH * 128, TSC], F32, kind=kind_dbg("ssm"))
    w1q_d = nc.dram_tensor("w1q_d", [NFC * 128, NDK * 128], BF16, kind="Internal")
    w2q_d = nc.dram_tensor("w2q_d", [FF, DH], BF16, kind="Internal")
    z_d = nc.dram_tensor("z_d", [NFK * NTAU * 128, TAU], BF16, kind="Internal")

    from contextlib import ExitStack
    ctx = ExitStack()
    SB = lambda n, s, d=F32: ctx.enter_context(nc.sbuf_tensor(n, s, d))
    PS = lambda n, s: ctx.enter_context(nc.psum_tensor(n, s, F32))
    SEM = lambda n: ctx.enter_context(nc.semaphore(n))

    with ctx:
        A_sb = SB("A_sb", [128, NJ * NN])
        D_sb = SB("D_sb", [128, NJ])
        dpb_sb = SB("dpb_sb", [128, NJ])
        b1_sb = SB("b1_sb", [128, NFC])
        b2h_sb = SB("b2h_sb", [128, NDM])
        dpT_r = SB("dpT_r", [RK, DH], F32R)
        xpT_b = SB("xpT_b", [128, NDK * 160], BF16)
        pat_r = SB("pat_r", [NN, NN * 128], F32R)
        ones_row = SB("ones_row", [1, 128])
        ones_sb = SB("ones_sb", [128, 1])
        state_sb = SB("state_sb", [128, NJ * NN])
        stat = [SB(f"stat{p}", [128, 8]) for p in range(2)]
        acc_w1 = SB("acc_w1", [128, NFC])
        acc_w2 = SB("acc_w2", [128, W2T_TILES])
        scl_sb = SB("scl_sb", [128, 4])
        sclp = SB("sclp", [1, 4])
        arena = SB("arena", [128, 21504])
        def carve(off_bytes, rows, width_elems, dt=F32):
            esz = 4 if dt in (F32, F32R) else 2
            a = arena[0:rows, off_bytes // 4:(off_bytes + width_elems * esz + 3) // 4]
            return a.bitcast(dt)[:, 0:width_elems] if dt not in (F32, F32R) else (
                a if dt == F32 else a.bitcast(F32R))
        x_t = [carve(0, 128, DM), carve(8192, 128, DM)]
        xn_t = [carve(16384, 128, DM, BF16), carve(20480, 128, DM, BF16)]
        aw = [carve(45056 + 8192 * p, 128, DM) for p in range(4)]
        wq_in = [carve(45056, 128, DM), carve(53248, 128, DM)]
        q_t2 = [carve(61440, 128, DM, BF16), carve(65536, 128, DM, BF16)]
        q_o = [carve(69632, 128, DM, BF16), carve(73728, 128, DM, BF16)]
        xdt = [SB(f"xdt{p}", [128, NDK * TSC], BF16) for p in range(2)]
        delta_b = SB("delta_b", [128, NJ * TSC])
        u_b = SB("u_b", [128, NJ * TSC])
        brep = SB("brep", [128, NN * TSC])
        crep = SB("crep", [128, NN * TSC])
        dtT_c = [SB(f"dtT_c{p}", [RK, TSC], F32R) for p in range(2)]
        Bf_c = [SB(f"Bf_c{p}", [NN, TSC], F32R) for p in range(2)]
        Cf_c = [SB(f"Cf_c{p}", [NN, TSC], F32R) for p in range(2)]
        dA_b = [SB(f"dA_b{p}", [128, TSC]) for p in range(8)]
        dBx_b = [SB(f"dBx_b{p}", [128, TSC]) for p in range(4)]
        h_b = [SB(f"h_b{p}", [128, TSC]) for p in range(8)]
        hc_b = [SB(f"hc_b{p}", [128, TSC]) for p in range(2)]
        y_acc = SB("y_acc", [128, TSC])
        s_t = [SB(f"s_t{p}", [128, TSC]) for p in range(8)]
        xpT_f = carve(24576, 128, NDK * 160)
        dpT_f = carve(35840, RK, DH)
        xnr = carve(0, 128, NDK * (NTQ * TAU), BF16)
        w1q_sb = [carve(32768 + 4096 * p, 128, NDK * 128, BF16) for p in range(3)]
        z_st = [carve(81920 + 1024 * p, 128, TAU, BF16) for p in range(4)]
        z_sl = [carve(8192 * p, 128, FKG * TAU, BF16) for p in range(ZRING)]
        w2q_sb = [carve(40960 + 2048 * p, 128, DH, BF16) for p in range(8)]
        z2 = [carve(57344 + 2048 * p, 128, TAU) for p in range(NDM)]
        s_in = [carve(73728 + 2048 * p, 128, TAU) for p in range(2)]
        xg_in = [carve(77824 + 2048 * p, 128, TAU) for p in range(2)]
        og = [carve(81920 + 2048 * p, 128, TAU) for p in range(2)]
        ps_mm = [PS(f"ps_mm{p}", [128, TAU]) for p in range(4)]
        ps_b4 = PS("ps_b4", [128, TAU])
        ps_b5 = PS("ps_b5", [128, TAU])
        ps_b6 = PS("ps_b6", [128, TAU])
        ps_b7 = PS("ps_b7", [128, TAU])

        names = ["sx", "sprm", "sln", "ssf", "svf", "sxnsp", "sprd",
                 "sscl_ld", "sabs", "smsc", "sscl2", "sscl3", "stsg", "sthr",
                 "stn_ld", "stn_v", "sqw",
                 "sxdt", "sdblm", "sdble", "sdelm", "sdel", "su",
                 "srepm", "srepe", "sdA", "sdBx", "sh", "shc", "sstt", "sssp", "sst",
                 "sw1q", "sxnr", "smm1", "szev", "szsp",
                 "szl", "sw2l", "sgel", "smm2", "sz2", "ssil",
                 "ssld", "sxld", "scmb", "sout", "svz",
                 "sx1", "sxdt1", "sscl1", "stn_ld1",
                 "szsp1", "szsp2", "szsp3", "sw2l1", "sw2l2", "sw2l3", "sw2l4", "sw2l5", "sw2l6", "sw2l7", "ssz", "sgz"]
        sem = {n: SEM(n) for n in names}

        pre_loads = {}
        if want_ffn:
            for th in range(1, NQH):
                key = th * NFC * NTQ - 1
                pre_loads.setdefault(key, []).append(("xnr", th))
                pre_loads.setdefault(key, []).append(("w1q", th, 0))
                pre_loads.setdefault(key, []).append(("w1q", th, 1))
            for g in range(NGR):
                th, rem = divmod(g, NFC * NTQ)
                fc, tq = divmod(rem, NTQ)
                if tq == 0 and fc >= 2 and g - 5 >= 0:
                    pre_loads.setdefault(g - 5, []).append(("w1q", th, fc))

        aslot = [sem["sscl_ld"], sem["sscl1"], sem["sscl2"], sem["sscl3"]]

        def abs_src(t):
            if t < NFC:
                return w1t_in[t * 128:(t + 1) * 128, :]
            r = t - NFC
            return w2T_in[r * 128:(r + 1) * 128, :]

        def _tern_load(eng, t):
            if t < NFC:
                fs, rb = divmod(t, 16)
                T = rb * 4 + fs
                srcap = w1t_in[T * 128:(T + 1) * 128, :]
                dst = wq_in[t % 2][:, :]
            else:
                r = t - NFC
                srcap = w2Th_in[r * 128:(r + 1) * 128, :]
                dst = wq_in[t % 2][:, 0:DH]
            stl = sem["stn_ld"] if t % 2 == 0 else sem["stn_ld1"]
            eng.dma_start(out=dst, in_=srcap).then_inc(stl, 16)

        with nc.Block() as blk:

            @blk.sync
            def _(eng):
                for t, src in [(A_sb, A_in), (D_sb, D_in), (dpb_sb, dpb_in),
                               (b1_sb, b1_in), (b2h_sb, b2h_in), (dpT_f, dpT_in),
                               (xpT_f, xpT_in), (pat_r, pat_in),
                               (ones_row, onesr_in), (ones_sb, ones_in)]:
                    eng.dma_start(out=t[:, :], in_=src[:, :]).then_inc(sem["sprm"], 16)
                for k in range(NLT + 2):
                    if k < NLT:
                        if k >= 2:
                            eng.wait_ge(sem["sln"], k - 1)
                        sxk = sem["sx"] if k % 2 == 0 else sem["sx1"]
                        eng.dma_start(out=x_t[k % 2][:, :],
                                      in_=x_tm[k * 128:(k + 1) * 128, :]).then_inc(sxk, 16)
                    if k >= 2:
                        ks = k - 2
                        eng.wait_ge(sem["sln"], ks + 1)
                        eng.wait_ge(sem["sxnsp"], 16 * ks)
                        eng.dma_start(out=xn_d[ks * 128:(ks + 1) * 128, :],
                                      in_=xn_t[ks % 2][:, :]).then_inc(sem["sxnsp"], 16)
                if not want_ssm:
                    eng.wait_ge(sem["sxnsp"], 16 * NLT)
                    return
                def xdt_load(c):
                    eng.wait_ge(sem["sxnsp"], 16 * min(NLT, 2 * c + 2))
                    if c >= 2:
                        eng.wait_ge(sem["sstt"], (c - 1) * NJ)
                    sxd = sem["sxdt"] if c % 2 == 0 else sem["sxdt1"]
                    for dk in range(NDK):
                        eng.dma_start(
                            out=xdt[c % 2][:, dk * TSC:(dk + 1) * TSC],
                            in_=xn_d[c * TSC:(c + 1) * TSC, dk * 128:(dk + 1) * 128],
                            transpose=True).then_inc(sxd, 16)

                xdt_load(0)
                for c in range(NCH):
                    if c >= 1:
                        for j in range(NJ):
                            idx = (c - 1) * NJ + j
                            eng.wait_ge(sem["sstt"], idx + 1)
                            eng.wait_ge(sem["sssp"], 16 * idx)
                            eng.dma_start(
                                out=s_d[(j * NCH + c - 1) * 128:(j * NCH + c) * 128, :],
                                in_=s_t[idx % 8][:, :]).then_inc(sem["sssp"], 16)
                    if c + 1 < NCH:
                        xdt_load(c + 1)
                for j in range(NJ):
                    idx = (NCH - 1) * NJ + j
                    eng.wait_ge(sem["sstt"], idx + 1)
                    eng.wait_ge(sem["sssp"], 16 * idx)
                    eng.dma_start(
                        out=s_d[(j * NCH + NCH - 1) * 128:(j * NCH + NCH) * 128, :],
                        in_=s_t[idx % 8][:, :]).then_inc(sem["sssp"], 16)
                if STAGE == "ssm":
                    eng.wait_ge(sem["sssp"], 16 * NCH * NJ)
                    return
                def sx_out_loop(tau):
                    for dm in range(NDM):
                        t = tau * NDM + dm
                        spl = (2 * tau + 1) * NJ + dm + 1
                        eng.wait_ge(sem["sssp"], 16 * spl)
                        if t >= 2:
                            eng.wait_ge(sem["scmb"], t - 1)
                        eng.wait_ge(sem["ssld"], 16 * t)
                        eng.dma_start(out=s_in[t % 2][:, :],
                                      in_=bass.AP(s_d, (dm * NCH + 2 * tau) * 128 * TSC,
                                                  [[TSC, 128], [128 * TSC, 2], [1, TSC]])
                                      ).then_inc(sem["ssld"], 16)
                        eng.wait_ge(sem["sxld"], 16 * t)
                        eng.dma_start(out=xg_in[t % 2][:, :],
                                      in_=x_dmj[(dm * NTAU + tau) * 128:(dm * NTAU + tau + 1) * 128, :]
                                      ).then_inc(sem["sxld"], 16)
                        eng.wait_ge(sem["scmb"], t + 1)
                        eng.wait_ge(sem["sout"], 16 * t)
                        eng.dma_start(out=out_dm[(dm * NTAU + tau) * 128:(dm * NTAU + tau + 1) * 128, :],
                                      in_=og[t % 2][:, :]).then_inc(sem["sout"], 16)

                eng.wait_ge(sem["smm1"], NGR)
                eng.wait_ge(sem["szev"], NGR)
                eng.wait_ge(sem["sqw"], 16 * NQT)
                for zss in (sem["szsp"], sem["szsp1"], sem["szsp2"], sem["szsp3"]):
                    eng.wait_ge(zss, 16 * (NGR // 4))
                for tau in range(NTAU):
                    for grp in range(NZG):
                        ri = tau * NZG + grp
                        if ri >= ZRING:
                            pri = ri - ZRING
                            ptau, pgrp = divmod(pri, NZG)
                            eng.wait_ge(sem["smm2"], ptau * NFK + (pgrp + 1) * FKG)
                        srcap = bass.AP(z_d, (grp * FKG * NTAU + tau) * 128 * TAU,
                                        [[TAU, 128], [NTAU * 128 * TAU, FKG], [1, TAU]])
                        eng.wait_ge(sem["szl"], 16 * ri)
                        eng.dma_start(out=z_sl[ri % ZRING][:, :], in_=srcap).then_inc(sem["szl"], 16)
                        for fk in range(grp * FKG, (grp + 1) * FKG):
                            idx = tau * NFK + fk
                            if idx >= 8:
                                eng.wait_ge(sem["smm2"], idx - 7)
                            w2s = [sem["sw2l"], sem["sw2l1"], sem["sw2l2"],
                                   sem["sw2l3"], sem["sw2l4"], sem["sw2l5"],
                                   sem["sw2l6"], sem["sw2l7"]][idx % 8]
                            eng.dma_start(out=w2q_sb[idx % 8][:, :],
                                          in_=w2q_d[fk * 128:(fk + 1) * 128, :]).then_inc(w2s, 16)
                    if tau >= 1:
                        sx_out_loop(tau - 1)
                sx_out_loop(NTAU - 1)
                eng.wait_ge(sem["sout"], 16 * NTAU * NDM)

            @blk.scalar
            def _(eng):
                sz = [0]
                def sfence(ii):
                    sz[0] += 1
                    ii.then_inc(sem["ssz"], 1)
                    eng.wait_ge(sem["ssz"], sz[0])
                eng.wait_ge(sem["sprm"], 16 * 10)
                for k in range(NLT):
                    st = stat[k % 2]
                    eng.wait_ge(sem["sx"] if k % 2 == 0 else sem["sx1"], 16 * (k // 2 + 1))
                    if k >= 2:
                        eng.wait_ge(sem["sxnsp"], 16 * (k - 1))
                        eng.wait_ge(sem["svf"], 2 * (k - 2) + 2)
                    eng.activation(xn_t[k % 2][:, :], x_t[k % 2][:, :], AF.Square,
                                   accum_out=st[:, 1:2]).then_inc(sem["ssz"], 1)
                    sz[0] += 1
                    eng.wait_ge(sem["ssz"], sz[0])
                    eng.activation(xn_t[k % 2][:, :], x_t[k % 2][:, :], AF.Identity,
                                   accum_out=st[:, 0:1]).then_inc(sem["ssf"], 1)
                    eng.wait_ge(sem["svf"], 2 * k + 1)
                    eng.activation(st[:, 5:6], st[:, 4:5], AF.Ln).then_inc(sem["ssz"], 1)
                    sz[0] += 1
                    eng.wait_ge(sem["ssz"], sz[0])
                    eng.activation(st[:, 5:6], st[:, 5:6], AF.Exp,
                                   scale=-0.5).then_inc(sem["ssf"], 1)
                    eng.wait_ge(sem["svf"], 2 * k + 2)
                    eng.activation(xn_t[k % 2][:, :], x_t[k % 2][:, :], AF.Identity,
                                   scale=st[:, 5:6], bias=st[:, 6:7]).then_inc(sem["sln"], 1)
                eng.activation(A_sb[:, :], A_sb[:, :], AF.Exp).then_inc(sem["ssf"], 1)

                def abs_load(t):
                    eng.dma_start(out=aw[t % 4][:, :],
                                  in_=abs_src(t)).then_inc(aslot[t % 4], 16)

                def abs_tile(t):
                    eng.wait_ge(aslot[t % 4], 16 * (t // 4 + 1))
                    acc = acc_w1[:, t:t + 1] if t < NFC else acc_w2[:, t - NFC:t - NFC + 1]
                    eng.activation(aw[t % 4][:, :], aw[t % 4][:, :], AF.Abs,
                                   accum_out=acc).then_inc(sem["sabs"], 1)
                    if t + 4 < NFC + W2T_TILES:
                        eng.wait_ge(sem["sabs"], t + 1)
                        abs_load(t + 4)

                def tern_load(t):
                    _tern_load(eng, t)

                def q_write(t):
                    eng.wait_ge(sem["stn_v"], t + 1)
                    eng.wait_ge(sem["sqw"], 16 * t)
                    if t < NFC:
                        fs, rb = divmod(t, 16)
                        dstap = bass.AP(w1q_d, (fs * 16) * 128 * (NDK * 128) + rb * 128,
                                        [[NDK * 128, 128], [128 * (NDK * 128), 16], [1, 128]])
                        eng.dma_start(out=dstap, in_=q_o[t % 2][:, :]).then_inc(sem["sqw"], 16)
                    else:
                        r = t - NFC
                        eng.dma_start(out=w2q_d[r * 128:(r + 1) * 128, :],
                                      in_=q_o[t % 2][:, 0:DH]).then_inc(sem["sqw"], 16)

                def tern_tile(t):
                    if t == 0:
                        eng.wait_ge(sem["sthr"], 1)
                        tern_load(0)
                        tern_load(1)
                    wsl = wq_in[t % 2][:, :] if t < NFC else wq_in[t % 2][:, 0:DH]
                    oq = q_o[t % 2][:, :] if t < NFC else q_o[t % 2][:, 0:DH]
                    o2 = q_t2[t % 2][:, :] if t < NFC else q_t2[t % 2][:, 0:DH]
                    nh = scl_sb[:, 2:3] if t < NFC else scl_sb[:, 3:4]
                    ph = scl_sb[:, 0:1] if t < NFC else scl_sb[:, 1:2]
                    eng.wait_ge(sem["stn_ld"] if t % 2 == 0 else sem["stn_ld1"],
                                16 * (t // 2 + 1))
                    if t >= 2:
                        eng.wait_ge(sem["sqw"], 16 * (t - 1))
                        eng.wait_ge(sem["stn_v"], t - 1)
                    eng.activation(oq, wsl, AF.Sign, bias=nh)
                    eng.activation(o2, wsl, AF.Sign, bias=ph).then_inc(sem["stsg"], 1)
                    if t + 2 < NQT:
                        eng.wait_ge(sem["stsg"], t + 1)
                        tern_load(t + 2)
                    if t >= 1:
                        q_write(t - 1)
                    if t == NQT - 1:
                        q_write(t)

                if want_ffn:
                    for t in range(4):
                        abs_load(t)
                at = [0]
                tt = [0]

                if want_ssm:
                    gz = 0
                    for c in range(NCH):
                        p = c % 2
                        eng.wait_ge(sem["sdblm"], 3 * (c + 1))
                        if c >= 2:
                            eng.wait_ge(sem["sdelm"], (c - 1) * NJ)
                            eng.wait_ge(sem["srepm"], (c - 1) * 2 * NN)
                        eng.copy(dtT_c[p][:, :], ps_b4[:, 0:TSC])
                        eng.copy(Bf_c[p][:, :], ps_b5[0:NN, 0:TSC])
                        eng.copy(Cf_c[p][:, :], ps_b5[0:NN, TSC:2 * TSC]).then_inc(sem["sdble"], 3)
                        for j in range(NJ):
                            eng.wait_ge(sem["sdelm"], c * NJ + j + 1)
                            if c >= 1:
                                eng.wait_ge(sem["su"], (c - 1) * NJ + j + 1)
                            dcol = delta_b[:, j * TSC:(j + 1) * TSC]
                            psrc = (ps_b4 if j % 2 == 0 else ps_b5)[:, 0:TSC]
                            sfence(eng.activation(dcol, psrc, AF.Exp, bias=dpb_sb[:, j:j + 1]))
                            eng.activation(dcol, dcol, AF.Ln, bias=1.0).then_inc(sem["sdel"], 1)
                        for n in range(NN):
                            eng.wait_ge(sem["srepm"], c * 2 * NN + 2 * (n + 1))
                            if c >= 1:
                                base = (c - 1) * 128 + 7 * NN
                                eng.wait_ge(sem["sdBx"], base + n + 1)
                                eng.wait_ge(sem["shc"], base + n + 1)
                            bank = ps_b6 if n % 2 == 0 else ps_b7
                            eng.copy(brep[:, n * TSC:(n + 1) * TSC], bank[:, 0:TSC])
                            eng.copy(crep[:, n * TSC:(n + 1) * TSC], bank[:, TSC:2 * TSC]).then_inc(sem["srepe"], 2)
                        if c == 0:
                            eng.wait_ge(sem["sprd"], 1)
                        nsp = zev[c]
                        spr_each = max(1, -(-nsp // NJ)) if nsp else 0
                        done_sp = 0
                        nab = ABS_SCHED.get(c, 0) if want_ffn else 0
                        for j in range(NJ):
                            for _ in range(nab // NJ):
                                abs_tile(at[0])
                                at[0] += 1
                            for n in range(NN):
                                k = c * 128 + j * NN + n
                                pp = k % 8
                                if k >= 8:
                                    eng.wait_ge(sem["sh"], k - 7)
                                eng.activation(dA_b[pp][:, :], delta_b[:, j * TSC:(j + 1) * TSC],
                                               AF.Exp,
                                               scale=A_sb[:, j * NN + n:j * NN + n + 1]).then_inc(sem["sdA"], 1)
                            if want_ffn:
                                todo = min(spr_each, nsp - done_sp) if j < NJ - 1 else nsp - done_sp
                                for _ in range(todo):
                                    g = gz
                                    gz += 1
                                    th, rem = divmod(g, NFC * NTQ)
                                    fc, tq = divmod(rem, NTQ)
                                    eng.wait_ge(sem["smm1"], g + 1)
                                    if g >= 4:
                                        zss = [sem["szsp"], sem["szsp1"], sem["szsp2"], sem["szsp3"]][g % 4]
                                        eng.wait_ge(zss, 16 * (g // 4))
                                    zb = z_st[g % 4]
                                    eng.activation(zb[:, :], ps_mm[g % 4][:, :], AF.Identity,
                                                   scale=0.5,
                                                   bias=b1_sb[:, fc:fc + 1]).then_inc(sem["szev"], 1)
                                    eng.wait_ge(sem["szev"], g + 1)
                                    tau_g = th * NTQ + tq
                                    zss = [sem["szsp"], sem["szsp1"], sem["szsp2"], sem["szsp3"]][g % 4]
                                    eng.dma_start(
                                        out=z_d[(fc * NTAU + tau_g) * 128:(fc * NTAU + tau_g + 1) * 128, :],
                                        in_=zb[:, :]).then_inc(zss, 16)
                                    for ld in pre_loads.get(g, []):
                                        if ld[0] == "w1q":
                                            _, lth, lfc = ld
                                            li = lth * NFC + lfc
                                            eng.wait_ge(sem["sqw"], 16 * 16 * (lfc // 16 + 1))
                                            eng.wait_ge(sem["smm1"], max(0, (li - 2) * NTQ))
                                            eng.wait_ge(sem["sw1q"], 16 * li)
                                            eng.dma_start(out=w1q_sb[li % 3][:, :],
                                                          in_=w1q_d[lfc * 128:(lfc + 1) * 128, :]
                                                          ).then_inc(sem["sw1q"], 16)
                                        else:
                                            _, lth = ld
                                            eng.wait_ge(sem["sxnsp"], 16 * NLT)
                                            eng.wait_ge(sem["smm1"], NFC * NTQ * lth)
                                            for dk in range(NDK):
                                                eng.dma_start(
                                                    out=xnr[:, dk * (NTQ * TAU):(dk + 1) * (NTQ * TAU)],
                                                    in_=xn_d[lth * NTQ * TAU:(lth + 1) * NTQ * TAU,
                                                             dk * 128:(dk + 1) * 128],
                                                    transpose=True).then_inc(sem["sxnr"], 16)
                                done_sp += todo
                        if want_ffn:
                            for _ in range(TERN_SCHED.get(c, 0)):
                                tern_tile(tt[0])
                                tt[0] += 1
                            if c == 7:
                                eng.wait_ge(sem["sxnsp"], 16 * NLT)
                                for dk in range(NDK):
                                    eng.dma_start(
                                        out=xnr[:, dk * (NTQ * TAU):(dk + 1) * (NTQ * TAU)],
                                        in_=xn_d[0:NTQ * TAU, dk * 128:(dk + 1) * 128],
                                        transpose=True).then_inc(sem["sxnr"], 16)
                                for fc0 in (0, 1, 2):
                                    eng.wait_ge(sem["sqw"], 16 * 16 * (fc0 // 16 + 1))
                                    eng.wait_ge(sem["sw1q"], 16 * fc0)
                                    eng.dma_start(out=w1q_sb[fc0 % 3][:, :],
                                                  in_=w1q_d[fc0 * 128:(fc0 + 1) * 128, :]
                                                  ).then_inc(sem["sw1q"], 16)

                if want_ffn:
                    def z2_silu(tau):
                        eng.wait_ge(sem["smm2"], (tau + 1) * NFK)
                        for dm in range(NDM):
                            bank = ps_mm[dm] if dm < 4 else [ps_b4, ps_b5, ps_b6, ps_b7][dm - 4]
                            if tau >= 1:
                                eng.wait_ge(sem["scmb"], (tau - 1) * NDM + dm + 1)
                            eng.activation(z2[dm][:, :], bank[:, :], AF.Identity,
                                           scale=0.5,
                                           bias=b2h_sb[:, dm:dm + 1]).then_inc(sem["sz2"], 1)
                        eng.wait_ge(sem["sz2"], (tau + 1) * NDM)
                        for dm in range(NDM):
                            eng.activation(z2[dm][:, :], z2[dm][:, :], AF.Silu).then_inc(sem["ssil"], 1)

                    for ri in range(NTAU * NZG):
                        eng.wait_ge(sem["szl"], 16 * (ri + 1))
                        zg = z_sl[ri % ZRING]
                        last = None
                        for fk in range(FKG):
                            last = eng.activation(zg[:, fk * TAU:(fk + 1) * TAU],
                                                  zg[:, fk * TAU:(fk + 1) * TAU], AF.Gelu)
                        last.then_inc(sem["sgel"], 1)
                        if ri % NZG == 1 and ri >= NZG:
                            z2_silu(ri // NZG - 1)
                    z2_silu(NTAU - 1)

            @blk.vector
            def _(eng):
                fz = [0]
                def fence(ii):
                    fz[0] += 1
                    ii.then_inc(sem["svz"], 1)
                    eng.wait_ge(sem["svz"], fz[0])
                for k in range(NLT):
                    st = stat[k % 2]
                    eng.wait_ge(sem["ssf"], 2 * k + 1)
                    fence(eng.tensor_scalar(st[:, 2:3], st[:, 0:1], 1.0 / DM, None, ALU.mult))
                    fence(eng.tensor_scalar(st[:, 3:4], st[:, 1:2], 1.0 / DM, None, ALU.mult))
                    fence(eng.tensor_tensor(st[:, 7:8], st[:, 2:3], st[:, 2:3], ALU.mult))
                    fence(eng.tensor_sub(st[:, 4:5], st[:, 3:4], st[:, 7:8]))
                    eng.tensor_scalar(st[:, 4:5], st[:, 4:5], EPS_LN, None, ALU.add).then_inc(sem["svf"], 1)
                    eng.wait_ge(sem["ssf"], 2 * k + 2)
                    fence(eng.tensor_tensor(st[:, 6:7], st[:, 2:3], st[:, 5:6], ALU.mult))
                    eng.tensor_scalar(st[:, 6:7], st[:, 6:7], -1.0, None, ALU.mult).then_inc(sem["svf"], 1)
                eng.wait_ge(sem["ssf"], 2 * NLT + 1)
                fence(eng.tensor_scalar(A_sb[:, :], A_sb[:, :], -1.0, None, ALU.mult))
                eng.tensor_copy(dpT_r[:, :], dpT_f[:, :])
                eng.tensor_copy(xpT_b[:, :], xpT_f[:, :])
                eng.memset(state_sb[:, :], 0.0)
                fence(eng.memset(y_acc[:, :], 0.0))
                eng.nop().then_inc(sem["sprd"], 1)
                vt = [0]

                if want_ssm:
                    for c in range(NCH):
                        p = c % 2
                        eng.wait_ge(sem["sxdt"] if c % 2 == 0 else sem["sxdt1"], 16 * NDK * (c // 2 + 1))
                        for j in range(NJ):
                            eng.wait_ge(sem["sdel"], c * NJ + j + 1)
                            if c >= 1:
                                eng.wait_ge(sem["sdBx"], (c - 1) * 128 + (j + 1) * NN)
                            eng.tensor_tensor(u_b[:, j * TSC:(j + 1) * TSC],
                                              delta_b[:, j * TSC:(j + 1) * TSC],
                                              xdt[p][:, j * TSC:(j + 1) * TSC],
                                              ALU.mult).then_inc(sem["su"], 1)
                        for j in range(NJ):
                            for n in range(NN):
                                k = c * 128 + j * NN + n
                                eng.wait_ge(sem["srepe"], c * 2 * NN + 2 * (n + 1))
                                if n == 0:
                                    eng.wait_ge(sem["su"], c * NJ + j + 1)
                                if k >= 4:
                                    eng.wait_ge(sem["sh"], k - 3)
                                eng.tensor_tensor(dBx_b[k % 4][:, :],
                                                  u_b[:, j * TSC:(j + 1) * TSC],
                                                  brep[:, n * TSC:(n + 1) * TSC],
                                                  ALU.mult).then_inc(sem["sdBx"], 1)
                                eng.wait_ge(sem["sdA"], k + 1)
                                eng.wait_ge(sem["sdBx"], k + 1)
                                if k >= 8:
                                    eng.wait_ge(sem["shc"], k - 7)
                                eng.tensor_tensor_scan(h_b[k % 8][:, :], dA_b[k % 8][:, :],
                                                       dBx_b[k % 4][:, :],
                                                       state_sb[:, j * NN + n:j * NN + n + 1],
                                                       ALU.mult, ALU.add).then_inc(sem["sh"], 1)
                                if k >= 4:
                                    kc = k - 4
                                    jc, nc2 = divmod(kc % 128, NN)
                                    jc += (kc // 128 != c) * 0
                                    jcc = (kc % 128) // NN
                                    eng.tensor_copy(state_sb[:, (kc % 128):(kc % 128) + 1],
                                                    h_b[kc % 8][:, TSC - 1:TSC]).then_inc(sem["sst"], 1)
                                if k == c * 128 + 127:
                                    for kc in range(k - 3, k + 1):
                                        eng.wait_ge(sem["sh"], kc + 1)
                                        eng.tensor_copy(state_sb[:, (kc % 128):(kc % 128) + 1],
                                                        h_b[kc % 8][:, TSC - 1:TSC]).then_inc(sem["sst"], 1)
                            eng.wait_ge(sem["shc"], c * 128 + j * NN + NN)
                            si = (c * NJ + j) % 8
                            if c * NJ + j >= 8:
                                eng.wait_ge(sem["sssp"], 16 * (c * NJ + j - 7))
                            eng.scalar_tensor_tensor(s_t[si][:, :],
                                                     xdt[p][:, j * TSC:(j + 1) * TSC],
                                                     D_sb[:, j:j + 1], y_acc[:, :],
                                                     ALU.mult, ALU.add).then_inc(sem["sstt"], 1)
                        if want_ffn:
                            if c == 4:
                                eng.wait_ge(sem["sabs"], NFC + W2T_TILES)
                                eng.reduce_sum(stat[0][:, 0:1], acc_w1[:, :],
                                               axis=mybir.AxisListType.X)
                                eng.reduce_sum(stat[0][:, 1:2], acc_w2[:, :],
                                               axis=mybir.AxisListType.X).then_inc(sem["sprd"], 1)
                            if c == 5:
                                eng.wait_ge(sem["smsc"], 1)
                                fence(eng.tensor_scalar(sclp[0:1, 0:1], ps_mm[0][0:1, 0:1],
                                                        1.0 / (DM * FF), None, ALU.mult))
                                fence(eng.tensor_scalar(sclp[0:1, 1:2], ps_mm[0][0:1, 1:2],
                                                        1.0 / (DM * FF), None, ALU.mult))
                                fence(eng.tensor_scalar(sclp[0:1, 0:1], sclp[0:1, 0:1],
                                                        EPS_Q, None, ALU.max))
                                fence(eng.tensor_scalar(sclp[0:1, 1:2], sclp[0:1, 1:2],
                                                        EPS_Q, None, ALU.max))
                                fence(eng.tensor_scalar(sclp[0:1, 2:3], sclp[0:1, 0:1],
                                                        0.5, None, ALU.mult))
                                fence(eng.tensor_scalar(sclp[0:1, 3:4], sclp[0:1, 1:2],
                                                        0.5, None, ALU.mult))
                                eng.nop().then_inc(sem["sprd"], 1)
                            if c == 6:
                                eng.wait_ge(sem["smsc"], 2)
                                fence(eng.tensor_copy(scl_sb[:, 0:2], ps_mm[1][:, 0:2]))
                                fence(eng.tensor_scalar(scl_sb[:, 2:3], scl_sb[:, 0:1],
                                                        -1.0, None, ALU.mult))
                                fence(eng.tensor_scalar(scl_sb[:, 3:4], scl_sb[:, 1:2],
                                                        -1.0, None, ALU.mult))
                                eng.nop().then_inc(sem["sthr"], 1)
                            for _ in range(TERN_SCHED.get(c, 0)):
                                t = vt[0]
                                vt[0] += 1
                                oq = q_o[t % 2][:, :] if t < NFC else q_o[t % 2][:, 0:DH]
                                o2 = q_t2[t % 2][:, :] if t < NFC else q_t2[t % 2][:, 0:DH]
                                eng.wait_ge(sem["stsg"], t + 1)
                                eng.tensor_tensor(oq, oq, o2, ALU.add).then_inc(sem["stn_v"], 1)

                if want_ffn:
                    for tau in range(NTAU):
                        for dm in range(NDM):
                            t = tau * NDM + dm
                            eng.wait_ge(sem["ssil"], t + 1)
                            eng.wait_ge(sem["ssld"], 16 * (t + 1))
                            eng.wait_ge(sem["sxld"], 16 * (t + 1))
                            if t >= 2:
                                eng.wait_ge(sem["sout"], 16 * (t - 1))
                            fence(eng.tensor_tensor(og[t % 2][:, :], s_in[t % 2][:, :],
                                                    z2[dm][:, :], ALU.mult))
                            eng.tensor_tensor(og[t % 2][:, :], og[t % 2][:, :],
                                              xg_in[t % 2][:, :], ALU.add).then_inc(sem["scmb"], 1)

            def _tensor_body(eng):
                eng.wait_ge(sem["sprm"], 16 * 10)
                eng.matmul(ps_mm[0][0:1, 0:1], ones_sb[0:1, 0:1], ones_sb[0:1, 0:1],
                           start=True, stop=True)

                gz = 0
                if want_ssm:
                    for c in range(NCH):
                        p = c % 2
                        eng.wait_ge(sem["sxdt"] if c % 2 == 0 else sem["sxdt1"], 16 * NDK * (c // 2 + 1))
                        if c == 0:
                            eng.wait_ge(sem["sprd"], 1)
                        if c >= 1:
                            eng.wait_ge(sem["sdel"], c * NJ)
                        if c >= 1:
                            eng.wait_ge(sem["sdble"], 3 * c)
                        for dk in range(NDK):
                            eng.matmul(ps_b4[:, 0:TSC],
                                       xpT_b[:, dk * 160:dk * 160 + 128],
                                       xdt[p][:, dk * TSC:(dk + 1) * TSC],
                                       start=(dk == 0), stop=(dk == NDK - 1))
                        for dk in range(NDK):
                            eng.matmul(ps_b5[0:NN, 0:TSC],
                                       xpT_b[:, dk * 160 + 128:dk * 160 + 144],
                                       xdt[p][:, dk * TSC:(dk + 1) * TSC],
                                       start=(dk == 0), stop=(dk == NDK - 1))
                        for dk in range(NDK):
                            mm = eng.matmul(ps_b5[0:NN, TSC:2 * TSC],
                                            xpT_b[:, dk * 160 + 144:dk * 160 + 160],
                                            xdt[p][:, dk * TSC:(dk + 1) * TSC],
                                            start=(dk == 0), stop=(dk == NDK - 1))
                        mm.then_inc(sem["sdblm"], 3)
                        eng.wait_ge(sem["sdble"], 3 * (c + 1))
                        for j in range(NJ):
                            if c * NJ + j >= 2:
                                eng.wait_ge(sem["sdel"], c * NJ + j - 1)
                            bank = ps_b4 if j % 2 == 0 else ps_b5
                            eng.matmul(bank[:, 0:TSC], dpT_r[:, j * 128:(j + 1) * 128],
                                       dtT_c[p][:, :], start=True, stop=True).then_inc(sem["sdelm"], 1)
                        nmm = zev[c]
                        done_mm = 0

                        def mm1_group():
                            nonlocal gz
                            g = gz
                            gz += 1
                            th, rem = divmod(g, NFC * NTQ)
                            fc, tq = divmod(rem, NTQ)
                            if tq == 0:
                                eng.wait_ge(sem["sw1q"], 16 * (th * NFC + fc + 1))
                            if fc == 0 and tq == 0:
                                eng.wait_ge(sem["sxnr"], 16 * NDK * (th + 1))
                            if g >= 4:
                                eng.wait_ge(sem["szev"], g - 3)
                            for dk in range(NDK):
                                mm = eng.matmul(ps_mm[g % 4][:, :],
                                                w1q_sb[(th * NFC + fc) % 3][:, dk * 128:(dk + 1) * 128],
                                                xnr[:, dk * (NTQ * TAU) + tq * TAU:dk * (NTQ * TAU) + (tq + 1) * TAU],
                                                start=(dk == 0), stop=(dk == NDK - 1))
                            mm.then_inc(sem["smm1"], 1)

                        for n in range(NN):
                            r = c * 2 * NN + 2 * n
                            if r >= 2:
                                eng.wait_ge(sem["srepe"], r - 1)
                            bank = ps_b6 if n % 2 == 0 else ps_b7
                            eng.matmul(bank[:, 0:TSC], pat_r[:, n * 128:(n + 1) * 128],
                                       Bf_c[p][:, :], start=True, stop=True)
                            eng.matmul(bank[:, TSC:2 * TSC], pat_r[:, n * 128:(n + 1) * 128],
                                       Cf_c[p][:, :], start=True, stop=True).then_inc(sem["srepm"], 2)
                            if want_ffn and n >= 1 and done_mm < min(4, nmm):
                                mm1_group()
                                done_mm += 1
                        if want_ffn:
                            while done_mm < nmm:
                                mm1_group()
                                done_mm += 1
                            if c == 5:
                                eng.wait_ge(sem["sprd"], 2)
                                eng.matmul(ps_mm[0][0:1, 0:1], stat[0][:, 0:1],
                                           ones_sb[:, 0:1], start=True, stop=True)
                                eng.matmul(ps_mm[0][0:1, 1:2], stat[0][:, 1:2],
                                           ones_sb[:, 0:1],
                                           start=True, stop=True).then_inc(sem["smsc"], 1)
                            if c == 6:
                                eng.wait_ge(sem["sprd"], 3)
                                eng.matmul(ps_mm[1][:, 0:1], ones_row[0:1, :],
                                           sclp[0:1, 2:3], start=True, stop=True)
                                eng.matmul(ps_mm[1][:, 1:2], ones_row[0:1, :],
                                           sclp[0:1, 3:4],
                                           start=True, stop=True).then_inc(sem["smsc"], 1)

                if want_ffn:
                    assert gz == NGR
                    eng.wait_ge(sem["sdel"], NCH * NJ)
                    eng.wait_ge(sem["srepe"], NCH * 2 * NN)
                    eng.wait_ge(sem["sdble"], 3 * NCH)
                    eng.wait_ge(sem["szev"], NGR)
                    banks = [ps_mm[0], ps_mm[1], ps_mm[2], ps_mm[3], ps_b4, ps_b5, ps_b6, ps_b7]
                    for tau in range(NTAU):
                        for grp in range(NZG):
                            ri = tau * NZG + grp
                            eng.wait_ge(sem["sgel"], ri + 1)
                            zg = z_sl[ri % ZRING]
                            for fkl in range(FKG):
                                fk = grp * FKG + fkl
                                idx = tau * NFK + fk
                                w2s = [sem["sw2l"], sem["sw2l1"], sem["sw2l2"],
                                       sem["sw2l3"], sem["sw2l4"], sem["sw2l5"],
                                       sem["sw2l6"], sem["sw2l7"]][idx % 8]
                                eng.wait_ge(w2s, 16 * (idx // 8 + 1))
                                if fk == 0:
                                    eng.wait_ge(sem["sz2"], tau * NDM)
                                first = fk == 0
                                last = fk == NFK - 1
                                for dm in range(NDM):
                                    mm = eng.matmul(banks[dm][:, :],
                                                    w2q_sb[idx % 8][:, dm * 128:(dm + 1) * 128],
                                                    zg[:, fkl * TAU:(fkl + 1) * TAU],
                                                    start=first, stop=last)
                                mm.then_inc(sem["smm2"], 1)

            blk.tensor(_tensor_body)

            def _gpsimd_body(eng):
                gq = [0]
                eng.wait_ge(sem["sprm"], 16 * 10)
                eng.memset(hc_b[0][:, 0:1], 0.0)
                if want_ssm:
                    for c in range(NCH):
                        for j in range(NJ):
                            if c * NJ + j >= 1:
                                eng.wait_ge(sem["sstt"], c * NJ + j)
                            for n in range(NN):
                                k = c * 128 + j * NN + n
                                pp = k % 8
                                ph = k % 2
                                eng.wait_ge(sem["sh"], k + 1)
                                eng.wait_ge(sem["srepe"], c * 2 * NN + 2 * (n + 1))
                                if n == 0:
                                    eng.tensor_tensor(y_acc[:, :], h_b[pp][:, :],
                                                      crep[:, n * TSC:(n + 1) * TSC],
                                                      ALU.mult).then_inc(sem["shc"], 1)
                                else:
                                    gq[0] += 1
                                    eng.tensor_tensor(hc_b[ph][:, :], h_b[pp][:, :],
                                                      crep[:, n * TSC:(n + 1) * TSC],
                                                      ALU.mult).then_inc(sem["sgz"], 1)
                                    eng.wait_ge(sem["sgz"], gq[0])
                                    eng.tensor_add(y_acc[:, :], y_acc[:, :],
                                                   hc_b[ph][:, :]).then_inc(sem["shc"], 1)

            blk.gpsimd(_gpsimd_body)

    return nc


def _prep_core_inputs(c, full):
    b, i = c // 2, c % 2
    dlo = i * DH
    perm = np.r_[0:DM] if i == 0 else np.r_[DH:DM, 0:DH]
    half = slice(dlo, dlo + DH)
    x = np.asarray(full["L0_attn_x"][b], dtype=np.float32)
    A_log = np.asarray(full["L0_attn_A_log"], dtype=np.float32)
    Dp = np.asarray(full["L0_attn_D_param"], dtype=np.float32)
    xp = np.asarray(full["L0_attn_x_proj_w"], dtype=np.float32)
    dp = np.asarray(full["L0_attn_dt_proj_w"], dtype=np.float32)
    dpb = np.asarray(full["L0_attn_dt_proj_b"], dtype=np.float32)
    w1 = np.asarray(full["L0_ffn_w1"], dtype=np.float32)
    b1 = np.asarray(full["L0_ffn_b1"], dtype=np.float32)
    w2 = np.asarray(full["L0_ffn_w2"], dtype=np.float32)
    b2 = np.asarray(full["L0_ffn_b2"], dtype=np.float32)

    A_sb = np.ascontiguousarray(
        A_log[half].reshape(NJ, 128, NN).transpose(1, 0, 2).reshape(128, NJ * NN))
    D_col = np.ascontiguousarray(Dp[half].reshape(NJ, 128).T)
    dpb_col = np.ascontiguousarray(dpb[half].reshape(NJ, 128).T)
    dpT = np.ascontiguousarray(dp[half].T)
    xpT = xp.T[perm]
    xpT_sb = np.ascontiguousarray(
        xpT.reshape(NDK, 128, 160).transpose(1, 0, 2).reshape(128, NDK * 160))
    pat = np.zeros((NN, NN * 128), dtype=np.float32)
    for n in range(NN):
        pat[n, n * 128:(n + 1) * 128] = 1.0
    return dict(
        pat_in=pat,
        onesr_in=np.ones((1, 128), dtype=np.float32),
        x_tm=np.ascontiguousarray(x[:, perm]),
        x_dmj=np.ascontiguousarray(
            x.T[half].reshape(NDM, 128, NTAU, TAU).transpose(0, 2, 1, 3)
            .reshape(NDM * NTAU * 128, TAU)),
        A_in=A_sb, D_in=D_col, dpb_in=dpb_col, dpT_in=dpT,
        xpT_in=xpT_sb,
        ones_in=np.ones((128, 1), dtype=np.float32),
        w1t_in=np.ascontiguousarray(
            w1.T[perm].reshape(NDK, 128, 4, 2048).transpose(0, 2, 1, 3)
            .reshape(NFC * 128, 2048)),
        w2T_in=np.ascontiguousarray(w2.T),
        w2Th_in=np.ascontiguousarray(w2.T[:, half]),
        b1_in=np.ascontiguousarray(b1.reshape(NFC, 128).T),
        b2h_in=np.ascontiguousarray(b2[half].reshape(NDM, 128).T),
    )


def kernel(**inputs):
    nc = build()
    in_maps = [_prep_core_inputs(c, inputs) for c in range(8)]
    res = run_bass_kernel_spmd(nc, in_maps, core_ids=list(range(8)))
    out = np.empty((B, L, DM), dtype=np.float32)
    for c in range(8):
        b, i = c // 2, c % 2
        o = res.results[c]["out_dm"].reshape(NDM, NTAU, 128, TAU)
        out[b][:, i * DH:(i + 1) * DH] = o.transpose(0, 2, 1, 3).reshape(DH, L).T
    return out


# version 16
# speedup vs baseline: 1.0659x; 1.0659x over previous
import numpy as np
import concourse.bass as bass
import concourse.mybir as mybir
from concourse.bass_utils import run_bass_kernel_spmd

F32 = mybir.dt.float32
F32R = mybir.dt.float32r
BF16 = mybir.dt.bfloat16
AF = mybir.ActivationFunctionType
ALU = mybir.AluOpType

B, L, DM = 4, 4096, 2048
DH, NJ, NN, RK, FF = 1024, 8, 16, 128, 8192
EPS_LN, EPS_Q = 1e-5, 1e-8

NLT = L // 128
TSC = 256
NCH = L // TSC
TAU = 512
NTAU = L // TAU
NQH = 4
NTQ = 2
NFC = FF // 128
NDK = DM // 128
NDM = DH // 128
NFK = FF // 128
NGR = NQH * NFC * NTQ
ZRING = 5
FKG = 8
NZG = NFK // FKG
W2T_TILES = FF // 128
NQT = NFC + NFK

STAGE = "full"
DBG = False

MM1_START_CH = 9
ABS_SCHED = {0: 32, 1: 32, 2: 32, 3: 32}
TERN_SCHED = {6: 22, 7: 21, 8: 21, 9: 10, 10: 9, 11: 9, 12: 9, 13: 9, 14: 9, 15: 9}


def _zev_sched():
    sched = [0] * NCH
    per = -(-NGR // (NCH - MM1_START_CH))
    left = NGR
    for c in range(MM1_START_CH, NCH):
        take = min(per, left)
        sched[c] = take
        left -= take
    assert left == 0
    return sched


def build():
    nc = bass.Bass("TRN2", target_bir_lowering=False)
    want_ssm = STAGE in ("ssm", "full")
    want_ffn = STAGE == "full"
    zev = _zev_sched() if want_ffn else [0] * NCH

    def kind_dbg(stage_match):
        return "ExternalOutput" if (DBG and STAGE == stage_match) else "Internal"

    x_tm = nc.dram_tensor("x_tm", [L, DM], F32, kind="ExternalInput")
    x_dmj = nc.dram_tensor("x_dmj", [NDM * NTAU * 128, TAU], F32, kind="ExternalInput")
    A_in = nc.dram_tensor("A_in", [128, NJ * NN], F32, kind="ExternalInput")
    D_in = nc.dram_tensor("D_in", [128, NJ], F32, kind="ExternalInput")
    dpb_in = nc.dram_tensor("dpb_in", [128, NJ], F32, kind="ExternalInput")
    dpT_in = nc.dram_tensor("dpT_in", [RK, DH], F32, kind="ExternalInput")
    xpT_in = nc.dram_tensor("xpT_in", [128, NDK * 160], F32, kind="ExternalInput")
    pat_in = nc.dram_tensor("pat_in", [NN, NN * 128], F32R, kind="ExternalInput")
    onesr_in = nc.dram_tensor("onesr_in", [1, 128], F32, kind="ExternalInput")
    ones_in = nc.dram_tensor("ones_in", [128, 1], F32, kind="ExternalInput")
    w1t_in = nc.dram_tensor("w1t_in", [NFC * 128, 2048], F32, kind="ExternalInput")
    w2T_in = nc.dram_tensor("w2T_in", [FF, DM], F32, kind="ExternalInput")
    w2Th_in = nc.dram_tensor("w2Th_in", [FF, DH], F32, kind="ExternalInput")
    b1_in = nc.dram_tensor("b1_in", [128, NFC], F32, kind="ExternalInput")
    b2h_in = nc.dram_tensor("b2h_in", [128, NDM], F32, kind="ExternalInput")

    out_dm = nc.dram_tensor("out_dm", [NDM * NTAU * 128, TAU], F32, kind="ExternalOutput")
    xn_d = nc.dram_tensor("xn_d", [L, DM], BF16, kind=kind_dbg("ln"))
    s_d = nc.dram_tensor("s_d", [NJ * NCH * 128, TSC], F32, kind=kind_dbg("ssm"))
    w1q_d = nc.dram_tensor("w1q_d", [NFC * 128, NDK * 128], BF16, kind="Internal")
    w2q_d = nc.dram_tensor("w2q_d", [FF, DH], BF16, kind="Internal")
    z_d = nc.dram_tensor("z_d", [NFK * NTAU * 128, TAU], BF16, kind="Internal")

    from contextlib import ExitStack
    ctx = ExitStack()
    SB = lambda n, s, d=F32: ctx.enter_context(nc.sbuf_tensor(n, s, d))
    PS = lambda n, s: ctx.enter_context(nc.psum_tensor(n, s, F32))
    SEM = lambda n: ctx.enter_context(nc.semaphore(n))

    with ctx:
        A_sb = SB("A_sb", [128, NJ * NN])
        D_sb = SB("D_sb", [128, NJ])
        dpb_sb = SB("dpb_sb", [128, NJ])
        b1_sb = SB("b1_sb", [128, NFC])
        b2h_sb = SB("b2h_sb", [128, NDM])
        dpT_r = SB("dpT_r", [RK, DH], F32R)
        xpT_b = SB("xpT_b", [128, NDK * 160], BF16)
        pat_r = SB("pat_r", [NN, NN * 128], F32R)
        ones_row = SB("ones_row", [1, 128])
        ones_sb = SB("ones_sb", [128, 1])
        state_sb = SB("state_sb", [128, NJ * NN])
        stat = [SB(f"stat{p}", [128, 8]) for p in range(2)]
        acc_w1 = SB("acc_w1", [128, NFC])
        acc_w2 = SB("acc_w2", [128, W2T_TILES])
        scl_sb = SB("scl_sb", [128, 4])
        sclp = SB("sclp", [1, 4])
        arena = SB("arena", [128, 21504])
        def carve(off_bytes, rows, width_elems, dt=F32):
            esz = 4 if dt in (F32, F32R) else 2
            a = arena[0:rows, off_bytes // 4:(off_bytes + width_elems * esz + 3) // 4]
            return a.bitcast(dt)[:, 0:width_elems] if dt not in (F32, F32R) else (
                a if dt == F32 else a.bitcast(F32R))
        x_t = [carve(0, 128, DM), carve(8192, 128, DM)]
        xn_t = [carve(16384, 128, DM, BF16), carve(20480, 128, DM, BF16)]
        aw = [carve(45056 + 8192 * p, 128, DM) for p in range(4)]
        wq_in = [carve(45056, 128, DM), carve(53248, 128, DM)]
        q_t2 = [carve(61440, 128, DM, BF16), carve(65536, 128, DM, BF16)]
        q_o = [carve(69632, 128, DM, BF16), carve(73728, 128, DM, BF16)]
        xdt = [SB(f"xdt{p}", [128, NDK * TSC], BF16) for p in range(2)]
        delta_b = SB("delta_b", [128, NJ * TSC])
        u_b = SB("u_b", [128, NJ * TSC])
        brep = SB("brep", [128, NN * TSC])
        crep = SB("crep", [128, NN * TSC])
        dtT_c = [SB(f"dtT_c{p}", [RK, TSC], F32R) for p in range(2)]
        Bf_c = [SB(f"Bf_c{p}", [NN, TSC], F32R) for p in range(2)]
        Cf_c = [SB(f"Cf_c{p}", [NN, TSC], F32R) for p in range(2)]
        dA_b = [SB(f"dA_b{p}", [128, TSC]) for p in range(8)]
        dBx_b = [SB(f"dBx_b{p}", [128, TSC]) for p in range(4)]
        h_b = [SB(f"h_b{p}", [128, TSC]) for p in range(8)]
        hc_b = [SB(f"hc_b{p}", [128, TSC]) for p in range(2)]
        y_acc = SB("y_acc", [128, TSC])
        s_t = [SB(f"s_t{p}", [128, TSC]) for p in range(8)]
        xpT_f = carve(24576, 128, NDK * 160)
        dpT_f = carve(35840, RK, DH)
        xnr = carve(0, 128, NDK * (NTQ * TAU), BF16)
        w1q_sb = [carve(32768 + 4096 * p, 128, NDK * 128, BF16) for p in range(3)]
        z_st = [carve(81920 + 1024 * p, 128, TAU, BF16) for p in range(4)]
        z_sl = [carve(8192 * p, 128, FKG * TAU, BF16) for p in range(ZRING)]
        w2q_sb = [carve(40960 + 2048 * p, 128, DH, BF16) for p in range(8)]
        z2 = [carve(57344 + 2048 * p, 128, TAU) for p in range(NDM)]
        z2_all = carve(57344, 128, NDM * TAU)
        s_in = [carve(73728 + 2048 * p, 128, TAU) for p in range(2)]
        xg_in = [carve(77824 + 2048 * p, 128, TAU) for p in range(2)]
        og = [carve(81920 + 2048 * p, 128, TAU) for p in range(2)]
        ps_mm = [PS(f"ps_mm{p}", [128, TAU]) for p in range(4)]
        ps_b4 = PS("ps_b4", [128, TAU])
        ps_b5 = PS("ps_b5", [128, TAU])
        ps_b6 = PS("ps_b6", [128, TAU])
        ps_b7 = PS("ps_b7", [128, TAU])

        names = ["sx", "sprm", "sln", "ssf", "svf", "sxnsp", "sprd",
                 "sscl_ld", "sabs", "smsc", "sscl2", "sscl3", "stsg", "sthr",
                 "stn_ld", "stn_v", "sqw",
                 "sxdt", "sdblm", "sdble", "sdelm", "sdel", "su",
                 "srepm", "srepe", "sdA", "sdBx", "sh", "shc", "sstt", "sssp", "sst",
                 "sw1q", "sxnr", "smm1", "szev", "szsp",
                 "szl", "sw2l", "sgel", "smm2", "sz2", "ssil",
                 "ssld", "sxld", "scmb", "sout", "svz",
                 "sx1", "sxdt1", "sscl1", "stn_ld1",
                 "szsp1", "szsp2", "szsp3", "sw2l1", "sw2l2", "sw2l3", "sw2l4", "sw2l5", "sw2l6", "sw2l7", "ssz", "sgz"]
        sem = {n: SEM(n) for n in names}

        pre_loads = {}
        if want_ffn:
            for th in range(1, NQH):
                key = th * NFC * NTQ - 1
                pre_loads.setdefault(key, []).append(("xnr", th))
                pre_loads.setdefault(key, []).append(("w1q", th, 0))
                pre_loads.setdefault(key, []).append(("w1q", th, 1))
            for g in range(NGR):
                th, rem = divmod(g, NFC * NTQ)
                fc, tq = divmod(rem, NTQ)
                if tq == 0 and fc >= 2 and g - 5 >= 0:
                    pre_loads.setdefault(g - 5, []).append(("w1q", th, fc))

        aslot = [sem["sscl_ld"], sem["sscl1"], sem["sscl2"], sem["sscl3"]]

        def abs_src(t):
            if t < NFC:
                return w1t_in[t * 128:(t + 1) * 128, :]
            r = t - NFC
            return w2T_in[r * 128:(r + 1) * 128, :]

        def _tern_load(eng, t):
            if t < NFC:
                fs, rb = divmod(t, 16)
                T = rb * 4 + fs
                srcap = w1t_in[T * 128:(T + 1) * 128, :]
                dst = wq_in[t % 2][:, :]
            else:
                r = t - NFC
                srcap = w2Th_in[r * 128:(r + 1) * 128, :]
                dst = wq_in[t % 2][:, 0:DH]
            stl = sem["stn_ld"] if t % 2 == 0 else sem["stn_ld1"]
            eng.dma_start(out=dst, in_=srcap).then_inc(stl, 16)

        with nc.Block() as blk:

            @blk.sync
            def _(eng):
                for t, src in [(A_sb, A_in), (D_sb, D_in), (dpb_sb, dpb_in),
                               (b1_sb, b1_in), (b2h_sb, b2h_in), (dpT_f, dpT_in),
                               (xpT_f, xpT_in), (pat_r, pat_in),
                               (ones_row, onesr_in), (ones_sb, ones_in)]:
                    eng.dma_start(out=t[:, :], in_=src[:, :]).then_inc(sem["sprm"], 16)
                for k in range(NLT + 2):
                    if k < NLT:
                        if k >= 2:
                            eng.wait_ge(sem["sln"], k - 1)
                        sxk = sem["sx"] if k % 2 == 0 else sem["sx1"]
                        eng.dma_start(out=x_t[k % 2][:, :],
                                      in_=x_tm[k * 128:(k + 1) * 128, :]).then_inc(sxk, 16)
                    if k >= 2:
                        ks = k - 2
                        eng.wait_ge(sem["sln"], ks + 1)
                        eng.wait_ge(sem["sxnsp"], 16 * ks)
                        eng.dma_start(out=xn_d[ks * 128:(ks + 1) * 128, :],
                                      in_=xn_t[ks % 2][:, :]).then_inc(sem["sxnsp"], 16)
                if not want_ssm:
                    eng.wait_ge(sem["sxnsp"], 16 * NLT)
                    return
                def xdt_load(c):
                    eng.wait_ge(sem["sxnsp"], 16 * min(NLT, 2 * c + 2))
                    if c >= 2:
                        eng.wait_ge(sem["sstt"], (c - 1) * NJ)
                    sxd = sem["sxdt"] if c % 2 == 0 else sem["sxdt1"]
                    for dk in range(NDK):
                        eng.dma_start(
                            out=xdt[c % 2][:, dk * TSC:(dk + 1) * TSC],
                            in_=xn_d[c * TSC:(c + 1) * TSC, dk * 128:(dk + 1) * 128],
                            transpose=True).then_inc(sxd, 16)

                xdt_load(0)
                for c in range(NCH):
                    if c >= 1:
                        for j in range(NJ):
                            idx = (c - 1) * NJ + j
                            eng.wait_ge(sem["sstt"], idx + 1)
                            eng.wait_ge(sem["sssp"], 16 * idx)
                            eng.dma_start(
                                out=s_d[(j * NCH + c - 1) * 128:(j * NCH + c) * 128, :],
                                in_=s_t[idx % 8][:, :]).then_inc(sem["sssp"], 16)
                    if c + 1 < NCH:
                        xdt_load(c + 1)
                for j in range(NJ):
                    idx = (NCH - 1) * NJ + j
                    eng.wait_ge(sem["sstt"], idx + 1)
                    eng.wait_ge(sem["sssp"], 16 * idx)
                    eng.dma_start(
                        out=s_d[(j * NCH + NCH - 1) * 128:(j * NCH + NCH) * 128, :],
                        in_=s_t[idx % 8][:, :]).then_inc(sem["sssp"], 16)
                if STAGE == "ssm":
                    eng.wait_ge(sem["sssp"], 16 * NCH * NJ)
                    return
                def sx_out_loop(tau):
                    for dm in range(NDM):
                        t = tau * NDM + dm
                        spl = (2 * tau + 1) * NJ + dm + 1
                        eng.wait_ge(sem["sssp"], 16 * spl)
                        if t >= 2:
                            eng.wait_ge(sem["scmb"], t - 1)
                        eng.wait_ge(sem["ssld"], 16 * t)
                        eng.dma_start(out=s_in[t % 2][:, :],
                                      in_=bass.AP(s_d, (dm * NCH + 2 * tau) * 128 * TSC,
                                                  [[TSC, 128], [128 * TSC, 2], [1, TSC]])
                                      ).then_inc(sem["ssld"], 16)
                        eng.wait_ge(sem["sxld"], 16 * t)
                        eng.dma_start(out=xg_in[t % 2][:, :],
                                      in_=x_dmj[(dm * NTAU + tau) * 128:(dm * NTAU + tau + 1) * 128, :]
                                      ).then_inc(sem["sxld"], 16)
                        eng.wait_ge(sem["scmb"], t + 1)
                        eng.wait_ge(sem["sout"], 16 * t)
                        eng.dma_start(out=out_dm[(dm * NTAU + tau) * 128:(dm * NTAU + tau + 1) * 128, :],
                                      in_=og[t % 2][:, :]).then_inc(sem["sout"], 16)

                eng.wait_ge(sem["smm1"], NGR)
                eng.wait_ge(sem["szev"], NGR)
                eng.wait_ge(sem["sqw"], 16 * NQT)
                for zss in (sem["szsp"], sem["szsp1"], sem["szsp2"], sem["szsp3"]):
                    eng.wait_ge(zss, 16 * (NGR // 4))
                for tau in range(NTAU):
                    for grp in range(NZG):
                        ri = tau * NZG + grp
                        if ri >= ZRING:
                            pri = ri - ZRING
                            ptau, pgrp = divmod(pri, NZG)
                            eng.wait_ge(sem["smm2"], ptau * NFK + (pgrp + 1) * FKG)
                        srcap = bass.AP(z_d, (grp * FKG * NTAU + tau) * 128 * TAU,
                                        [[TAU, 128], [NTAU * 128 * TAU, FKG], [1, TAU]])
                        eng.wait_ge(sem["szl"], 16 * ri)
                        eng.dma_start(out=z_sl[ri % ZRING][:, :], in_=srcap).then_inc(sem["szl"], 16)
                        for fk in range(grp * FKG, (grp + 1) * FKG):
                            idx = tau * NFK + fk
                            if idx >= 8:
                                eng.wait_ge(sem["smm2"], idx - 7)
                            w2s = [sem["sw2l"], sem["sw2l1"], sem["sw2l2"],
                                   sem["sw2l3"], sem["sw2l4"], sem["sw2l5"],
                                   sem["sw2l6"], sem["sw2l7"]][idx % 8]
                            eng.dma_start(out=w2q_sb[idx % 8][:, :],
                                          in_=w2q_d[fk * 128:(fk + 1) * 128, :]).then_inc(w2s, 16)
                    if tau >= 1:
                        sx_out_loop(tau - 1)
                sx_out_loop(NTAU - 1)
                eng.wait_ge(sem["sout"], 16 * NTAU * NDM)

            @blk.scalar
            def _(eng):
                sz = [0]
                def sfence(ii):
                    sz[0] += 1
                    ii.then_inc(sem["ssz"], 1)
                    eng.wait_ge(sem["ssz"], sz[0])
                eng.wait_ge(sem["sprm"], 16 * 10)
                for k in range(NLT):
                    st = stat[k % 2]
                    eng.wait_ge(sem["sx"] if k % 2 == 0 else sem["sx1"], 16 * (k // 2 + 1))
                    if k >= 2:
                        eng.wait_ge(sem["sxnsp"], 16 * (k - 1))
                        eng.wait_ge(sem["svf"], 2 * (k - 2) + 2)
                    eng.activation(xn_t[k % 2][:, :], x_t[k % 2][:, :], AF.Square,
                                   accum_out=st[:, 1:2]).then_inc(sem["ssz"], 1)
                    sz[0] += 1
                    eng.wait_ge(sem["ssz"], sz[0])
                    eng.activation(xn_t[k % 2][:, :], x_t[k % 2][:, :], AF.Identity,
                                   accum_out=st[:, 0:1]).then_inc(sem["ssf"], 1)
                    eng.wait_ge(sem["svf"], 2 * k + 1)
                    eng.activation(st[:, 5:6], st[:, 4:5], AF.Ln).then_inc(sem["ssz"], 1)
                    sz[0] += 1
                    eng.wait_ge(sem["ssz"], sz[0])
                    eng.activation(st[:, 5:6], st[:, 5:6], AF.Exp,
                                   scale=-0.5).then_inc(sem["ssf"], 1)
                    eng.wait_ge(sem["svf"], 2 * k + 2)
                    eng.activation(xn_t[k % 2][:, :], x_t[k % 2][:, :], AF.Identity,
                                   scale=st[:, 5:6], bias=st[:, 6:7]).then_inc(sem["sln"], 1)
                eng.activation(A_sb[:, :], A_sb[:, :], AF.Exp).then_inc(sem["ssf"], 1)

                def abs_load(t):
                    eng.dma_start(out=aw[t % 4][:, :],
                                  in_=abs_src(t)).then_inc(aslot[t % 4], 16)

                def abs_tile(t):
                    eng.wait_ge(aslot[t % 4], 16 * (t // 4 + 1))
                    acc = acc_w1[:, t:t + 1] if t < NFC else acc_w2[:, t - NFC:t - NFC + 1]
                    eng.activation(aw[t % 4][:, :], aw[t % 4][:, :], AF.Abs,
                                   accum_out=acc).then_inc(sem["sabs"], 1)
                    if t + 4 < NFC + W2T_TILES:
                        eng.wait_ge(sem["sabs"], t + 1)
                        abs_load(t + 4)

                def tern_load(t):
                    _tern_load(eng, t)

                def q_write(t):
                    eng.wait_ge(sem["stn_v"], t + 1)
                    eng.wait_ge(sem["sqw"], 16 * t)
                    if t < NFC:
                        fs, rb = divmod(t, 16)
                        dstap = bass.AP(w1q_d, (fs * 16) * 128 * (NDK * 128) + rb * 128,
                                        [[NDK * 128, 128], [128 * (NDK * 128), 16], [1, 128]])
                        eng.dma_start(out=dstap, in_=q_o[t % 2][:, :]).then_inc(sem["sqw"], 16)
                    else:
                        r = t - NFC
                        eng.dma_start(out=w2q_d[r * 128:(r + 1) * 128, :],
                                      in_=q_o[t % 2][:, 0:DH]).then_inc(sem["sqw"], 16)

                def tern_tile(t):
                    if t == 0:
                        eng.wait_ge(sem["sthr"], 1)
                        tern_load(0)
                        tern_load(1)
                    if t >= 1:
                        q_write(t - 1)
                    wsl = wq_in[t % 2][:, :] if t < NFC else wq_in[t % 2][:, 0:DH]
                    oq = q_o[t % 2][:, :] if t < NFC else q_o[t % 2][:, 0:DH]
                    o2 = q_t2[t % 2][:, :] if t < NFC else q_t2[t % 2][:, 0:DH]
                    nh = scl_sb[:, 2:3] if t < NFC else scl_sb[:, 3:4]
                    ph = scl_sb[:, 0:1] if t < NFC else scl_sb[:, 1:2]
                    eng.wait_ge(sem["stn_ld"] if t % 2 == 0 else sem["stn_ld1"],
                                16 * (t // 2 + 1))
                    eng.activation(oq, wsl, AF.Sign, bias=nh)
                    eng.activation(o2, wsl, AF.Sign, bias=ph).then_inc(sem["stsg"], 1)
                    if t + 2 < NQT:
                        eng.wait_ge(sem["stsg"], t + 1)
                        tern_load(t + 2)
                    if t == NQT - 1:
                        q_write(t)

                if want_ffn:
                    for t in range(4):
                        abs_load(t)
                at = [0]
                tt = [0]

                if want_ssm:
                    gz = 0
                    for c in range(NCH):
                        p = c % 2
                        eng.wait_ge(sem["sdblm"], 3 * (c + 1))
                        if c >= 2:
                            eng.wait_ge(sem["sdelm"], (c - 1) * NJ)
                            eng.wait_ge(sem["srepm"], (c - 1) * 2 * NN)
                        eng.copy(dtT_c[p][:, :], ps_b4[:, 0:TSC])
                        eng.copy(Bf_c[p][:, :], ps_b5[0:NN, 0:TSC])
                        eng.copy(Cf_c[p][:, :], ps_b5[0:NN, TSC:2 * TSC]).then_inc(sem["sdble"], 3)
                        for j in range(NJ):
                            eng.wait_ge(sem["sdelm"], c * NJ + j + 1)
                            if c >= 1:
                                eng.wait_ge(sem["su"], (c - 1) * NJ + j + 1)
                            dcol = delta_b[:, j * TSC:(j + 1) * TSC]
                            psrc = (ps_b4 if j % 2 == 0 else ps_b5)[:, 0:TSC]
                            sfence(eng.activation(dcol, psrc, AF.Exp, bias=dpb_sb[:, j:j + 1]))
                            eng.activation(dcol, dcol, AF.Ln, bias=1.0).then_inc(sem["sdel"], 1)
                        for n in range(NN):
                            eng.wait_ge(sem["srepm"], c * 2 * NN + 2 * (n + 1))
                            if c >= 1:
                                base = (c - 1) * 128 + 7 * NN
                                eng.wait_ge(sem["sdBx"], base + n + 1)
                                eng.wait_ge(sem["shc"], base + n + 1)
                            bank = ps_b6 if n % 2 == 0 else ps_b7
                            eng.copy(brep[:, n * TSC:(n + 1) * TSC], bank[:, 0:TSC])
                            eng.copy(crep[:, n * TSC:(n + 1) * TSC], bank[:, TSC:2 * TSC]).then_inc(sem["srepe"], 2)
                        if c == 0:
                            eng.wait_ge(sem["sprd"], 1)
                        nsp = zev[c]
                        spr_each = max(1, -(-nsp // NJ)) if nsp else 0
                        done_sp = 0
                        nab = ABS_SCHED.get(c, 0) if want_ffn else 0
                        for j in range(NJ):
                            for _ in range(nab // NJ):
                                abs_tile(at[0])
                                at[0] += 1
                            for n in range(NN):
                                k = c * 128 + j * NN + n
                                pp = k % 8
                                if k >= 8:
                                    eng.wait_ge(sem["sh"], k - 7)
                                eng.activation(dA_b[pp][:, :], delta_b[:, j * TSC:(j + 1) * TSC],
                                               AF.Exp,
                                               scale=A_sb[:, j * NN + n:j * NN + n + 1]).then_inc(sem["sdA"], 1)
                            if want_ffn:
                                todo = min(spr_each, nsp - done_sp) if j < NJ - 1 else nsp - done_sp
                                for _ in range(todo):
                                    g = gz
                                    gz += 1
                                    th, rem = divmod(g, NFC * NTQ)
                                    fc, tq = divmod(rem, NTQ)
                                    eng.wait_ge(sem["smm1"], g + 1)
                                    if g >= 4:
                                        zss = [sem["szsp"], sem["szsp1"], sem["szsp2"], sem["szsp3"]][g % 4]
                                        eng.wait_ge(zss, 16 * (g // 4))
                                    zb = z_st[g % 4]
                                    eng.activation(zb[:, :], ps_mm[g % 4][:, :], AF.Identity,
                                                   scale=0.5,
                                                   bias=b1_sb[:, fc:fc + 1]).then_inc(sem["szev"], 1)
                                    eng.wait_ge(sem["szev"], g + 1)
                                    tau_g = th * NTQ + tq
                                    zss = [sem["szsp"], sem["szsp1"], sem["szsp2"], sem["szsp3"]][g % 4]
                                    eng.dma_start(
                                        out=z_d[(fc * NTAU + tau_g) * 128:(fc * NTAU + tau_g + 1) * 128, :],
                                        in_=zb[:, :]).then_inc(zss, 16)
                                    for ld in pre_loads.get(g, []):
                                        if ld[0] == "w1q":
                                            _, lth, lfc = ld
                                            li = lth * NFC + lfc
                                            eng.wait_ge(sem["sqw"], 16 * 16 * (lfc // 16 + 1))
                                            eng.wait_ge(sem["smm1"], max(0, (li - 2) * NTQ))
                                            eng.wait_ge(sem["sw1q"], 16 * li)
                                            eng.dma_start(out=w1q_sb[li % 3][:, :],
                                                          in_=w1q_d[lfc * 128:(lfc + 1) * 128, :]
                                                          ).then_inc(sem["sw1q"], 16)
                                        else:
                                            _, lth = ld
                                            eng.wait_ge(sem["sxnsp"], 16 * NLT)
                                            eng.wait_ge(sem["smm1"], NFC * NTQ * lth)
                                            for dk in range(NDK):
                                                eng.dma_start(
                                                    out=xnr[:, dk * (NTQ * TAU):(dk + 1) * (NTQ * TAU)],
                                                    in_=xn_d[lth * NTQ * TAU:(lth + 1) * NTQ * TAU,
                                                             dk * 128:(dk + 1) * 128],
                                                    transpose=True).then_inc(sem["sxnr"], 16)
                                done_sp += todo
                        if want_ffn:
                            for _ in range(TERN_SCHED.get(c, 0)):
                                tern_tile(tt[0])
                                tt[0] += 1
                            if c == 7:
                                eng.wait_ge(sem["sxnsp"], 16 * NLT)
                                for dk in range(NDK):
                                    eng.dma_start(
                                        out=xnr[:, dk * (NTQ * TAU):(dk + 1) * (NTQ * TAU)],
                                        in_=xn_d[0:NTQ * TAU, dk * 128:(dk + 1) * 128],
                                        transpose=True).then_inc(sem["sxnr"], 16)
                                for fc0 in (0, 1, 2):
                                    eng.wait_ge(sem["sqw"], 16 * 16 * (fc0 // 16 + 1))
                                    eng.wait_ge(sem["sw1q"], 16 * fc0)
                                    eng.dma_start(out=w1q_sb[fc0 % 3][:, :],
                                                  in_=w1q_d[fc0 * 128:(fc0 + 1) * 128, :]
                                                  ).then_inc(sem["sw1q"], 16)

                if want_ffn:
                    def z2_silu(tau):
                        eng.wait_ge(sem["smm2"], (tau + 1) * NFK)
                        for dm in range(NDM):
                            bank = ps_mm[dm] if dm < 4 else [ps_b4, ps_b5, ps_b6, ps_b7][dm - 4]
                            if tau >= 1:
                                eng.wait_ge(sem["scmb"], (tau - 1) * NDM + dm + 1)
                            eng.activation(z2[dm][:, :], bank[:, :], AF.Identity,
                                           scale=0.5,
                                           bias=b2h_sb[:, dm:dm + 1]).then_inc(sem["sz2"], 1)
                        eng.wait_ge(sem["sz2"], (tau + 1) * NDM)
                        eng.activation(z2_all[:, :], z2_all[:, :],
                                       AF.Silu).then_inc(sem["ssil"], NDM)

                    for ri in range(NTAU * NZG):
                        eng.wait_ge(sem["szl"], 16 * (ri + 1))
                        zg = z_sl[ri % ZRING]
                        eng.activation(zg[:, :], zg[:, :],
                                       AF.Gelu).then_inc(sem["sgel"], 1)
                        if ri % NZG == 1 and ri >= NZG:
                            z2_silu(ri // NZG - 1)
                    z2_silu(NTAU - 1)

            @blk.vector
            def _(eng):
                fz = [0]
                def fence(ii):
                    fz[0] += 1
                    ii.then_inc(sem["svz"], 1)
                    eng.wait_ge(sem["svz"], fz[0])
                for k in range(NLT):
                    st = stat[k % 2]
                    eng.wait_ge(sem["ssf"], 2 * k + 1)
                    fence(eng.tensor_scalar(st[:, 2:3], st[:, 0:1], 1.0 / DM, None, ALU.mult))
                    fence(eng.tensor_scalar(st[:, 3:4], st[:, 1:2], 1.0 / DM, None, ALU.mult))
                    fence(eng.tensor_tensor(st[:, 7:8], st[:, 2:3], st[:, 2:3], ALU.mult))
                    fence(eng.tensor_sub(st[:, 4:5], st[:, 3:4], st[:, 7:8]))
                    eng.tensor_scalar(st[:, 4:5], st[:, 4:5], EPS_LN, None, ALU.add).then_inc(sem["svf"], 1)
                    eng.wait_ge(sem["ssf"], 2 * k + 2)
                    fence(eng.tensor_tensor(st[:, 6:7], st[:, 2:3], st[:, 5:6], ALU.mult))
                    eng.tensor_scalar(st[:, 6:7], st[:, 6:7], -1.0, None, ALU.mult).then_inc(sem["svf"], 1)
                eng.wait_ge(sem["ssf"], 2 * NLT + 1)
                fence(eng.tensor_scalar(A_sb[:, :], A_sb[:, :], -1.0, None, ALU.mult))
                eng.tensor_copy(dpT_r[:, :], dpT_f[:, :])
                eng.tensor_copy(xpT_b[:, :], xpT_f[:, :])
                eng.memset(state_sb[:, :], 0.0)
                fence(eng.memset(y_acc[:, :], 0.0))
                eng.nop().then_inc(sem["sprd"], 1)
                vt = [0]

                if want_ssm:
                    for c in range(NCH):
                        p = c % 2
                        eng.wait_ge(sem["sxdt"] if c % 2 == 0 else sem["sxdt1"], 16 * NDK * (c // 2 + 1))
                        for j in range(NJ):
                            eng.wait_ge(sem["sdel"], c * NJ + j + 1)
                            if c >= 1:
                                eng.wait_ge(sem["sdBx"], (c - 1) * 128 + (j + 1) * NN)
                            eng.tensor_tensor(u_b[:, j * TSC:(j + 1) * TSC],
                                              delta_b[:, j * TSC:(j + 1) * TSC],
                                              xdt[p][:, j * TSC:(j + 1) * TSC],
                                              ALU.mult).then_inc(sem["su"], 1)
                        for j in range(NJ):
                            for n in range(NN):
                                k = c * 128 + j * NN + n
                                eng.wait_ge(sem["srepe"], c * 2 * NN + 2 * (n + 1))
                                if n == 0:
                                    eng.wait_ge(sem["su"], c * NJ + j + 1)
                                if k >= 4:
                                    eng.wait_ge(sem["sh"], k - 3)
                                eng.tensor_tensor(dBx_b[k % 4][:, :],
                                                  u_b[:, j * TSC:(j + 1) * TSC],
                                                  brep[:, n * TSC:(n + 1) * TSC],
                                                  ALU.mult).then_inc(sem["sdBx"], 1)
                                eng.wait_ge(sem["sdA"], k + 1)
                                eng.wait_ge(sem["sdBx"], k + 1)
                                if k >= 8:
                                    eng.wait_ge(sem["shc"], k - 7)
                                eng.tensor_tensor_scan(h_b[k % 8][:, :], dA_b[k % 8][:, :],
                                                       dBx_b[k % 4][:, :],
                                                       state_sb[:, j * NN + n:j * NN + n + 1],
                                                       ALU.mult, ALU.add).then_inc(sem["sh"], 1)
                                if k >= 4:
                                    kc = k - 4
                                    jc, nc2 = divmod(kc % 128, NN)
                                    jc += (kc // 128 != c) * 0
                                    jcc = (kc % 128) // NN
                                    eng.tensor_copy(state_sb[:, (kc % 128):(kc % 128) + 1],
                                                    h_b[kc % 8][:, TSC - 1:TSC]).then_inc(sem["sst"], 1)
                                if k == c * 128 + 127:
                                    for kc in range(k - 3, k + 1):
                                        eng.wait_ge(sem["sh"], kc + 1)
                                        eng.tensor_copy(state_sb[:, (kc % 128):(kc % 128) + 1],
                                                        h_b[kc % 8][:, TSC - 1:TSC]).then_inc(sem["sst"], 1)
                            eng.wait_ge(sem["shc"], c * 128 + j * NN + NN)
                            si = (c * NJ + j) % 8
                            if c * NJ + j >= 8:
                                eng.wait_ge(sem["sssp"], 16 * (c * NJ + j - 7))
                            eng.scalar_tensor_tensor(s_t[si][:, :],
                                                     xdt[p][:, j * TSC:(j + 1) * TSC],
                                                     D_sb[:, j:j + 1], y_acc[:, :],
                                                     ALU.mult, ALU.add).then_inc(sem["sstt"], 1)
                        if want_ffn:
                            if c == 4:
                                eng.wait_ge(sem["sabs"], NFC + W2T_TILES)
                                eng.reduce_sum(stat[0][:, 0:1], acc_w1[:, :],
                                               axis=mybir.AxisListType.X)
                                eng.reduce_sum(stat[0][:, 1:2], acc_w2[:, :],
                                               axis=mybir.AxisListType.X).then_inc(sem["sprd"], 1)
                            if c == 5:
                                eng.wait_ge(sem["smsc"], 1)
                                fence(eng.tensor_scalar(sclp[0:1, 0:1], ps_mm[0][0:1, 0:1],
                                                        1.0 / (DM * FF), None, ALU.mult))
                                fence(eng.tensor_scalar(sclp[0:1, 1:2], ps_mm[0][0:1, 1:2],
                                                        1.0 / (DM * FF), None, ALU.mult))
                                fence(eng.tensor_scalar(sclp[0:1, 0:1], sclp[0:1, 0:1],
                                                        EPS_Q, None, ALU.max))
                                fence(eng.tensor_scalar(sclp[0:1, 1:2], sclp[0:1, 1:2],
                                                        EPS_Q, None, ALU.max))
                                fence(eng.tensor_scalar(sclp[0:1, 2:3], sclp[0:1, 0:1],
                                                        0.5, None, ALU.mult))
                                fence(eng.tensor_scalar(sclp[0:1, 3:4], sclp[0:1, 1:2],
                                                        0.5, None, ALU.mult))
                                eng.nop().then_inc(sem["sprd"], 1)
                            if c == 6:
                                eng.wait_ge(sem["smsc"], 2)
                                fence(eng.tensor_copy(scl_sb[:, 0:2], ps_mm[1][:, 0:2]))
                                fence(eng.tensor_scalar(scl_sb[:, 2:3], scl_sb[:, 0:1],
                                                        -1.0, None, ALU.mult))
                                fence(eng.tensor_scalar(scl_sb[:, 3:4], scl_sb[:, 1:2],
                                                        -1.0, None, ALU.mult))
                                eng.nop().then_inc(sem["sthr"], 1)
                            for _ in range(TERN_SCHED.get(c, 0)):
                                t = vt[0]
                                vt[0] += 1
                                oq = q_o[t % 2][:, :] if t < NFC else q_o[t % 2][:, 0:DH]
                                o2 = q_t2[t % 2][:, :] if t < NFC else q_t2[t % 2][:, 0:DH]
                                eng.wait_ge(sem["stsg"], t + 1)
                                eng.tensor_tensor(oq, oq, o2, ALU.add).then_inc(sem["stn_v"], 1)

                if want_ffn:
                    for tau in range(NTAU):
                        for dm in range(NDM):
                            t = tau * NDM + dm
                            eng.wait_ge(sem["ssil"], t + 1)
                            eng.wait_ge(sem["ssld"], 16 * (t + 1))
                            eng.wait_ge(sem["sxld"], 16 * (t + 1))
                            if t >= 2:
                                eng.wait_ge(sem["sout"], 16 * (t - 1))
                            fence(eng.tensor_tensor(og[t % 2][:, :], s_in[t % 2][:, :],
                                                    z2[dm][:, :], ALU.mult))
                            eng.tensor_tensor(og[t % 2][:, :], og[t % 2][:, :],
                                              xg_in[t % 2][:, :], ALU.add).then_inc(sem["scmb"], 1)

            def _tensor_body(eng):
                eng.wait_ge(sem["sprm"], 16 * 10)
                eng.matmul(ps_mm[0][0:1, 0:1], ones_sb[0:1, 0:1], ones_sb[0:1, 0:1],
                           start=True, stop=True)

                gz = 0
                if want_ssm:
                    for c in range(NCH):
                        p = c % 2
                        eng.wait_ge(sem["sxdt"] if c % 2 == 0 else sem["sxdt1"], 16 * NDK * (c // 2 + 1))
                        if c == 0:
                            eng.wait_ge(sem["sprd"], 1)
                        if c >= 1:
                            eng.wait_ge(sem["sdel"], c * NJ)
                        if c >= 1:
                            eng.wait_ge(sem["sdble"], 3 * c)
                        for dk in range(NDK):
                            eng.matmul(ps_b4[:, 0:TSC],
                                       xpT_b[:, dk * 160:dk * 160 + 128],
                                       xdt[p][:, dk * TSC:(dk + 1) * TSC],
                                       start=(dk == 0), stop=(dk == NDK - 1))
                        for dk in range(NDK):
                            eng.matmul(ps_b5[0:NN, 0:TSC],
                                       xpT_b[:, dk * 160 + 128:dk * 160 + 144],
                                       xdt[p][:, dk * TSC:(dk + 1) * TSC],
                                       start=(dk == 0), stop=(dk == NDK - 1))
                        for dk in range(NDK):
                            mm = eng.matmul(ps_b5[0:NN, TSC:2 * TSC],
                                            xpT_b[:, dk * 160 + 144:dk * 160 + 160],
                                            xdt[p][:, dk * TSC:(dk + 1) * TSC],
                                            start=(dk == 0), stop=(dk == NDK - 1))
                        mm.then_inc(sem["sdblm"], 3)
                        eng.wait_ge(sem["sdble"], 3 * (c + 1))
                        for j in range(NJ):
                            if c * NJ + j >= 2:
                                eng.wait_ge(sem["sdel"], c * NJ + j - 1)
                            bank = ps_b4 if j % 2 == 0 else ps_b5
                            eng.matmul(bank[:, 0:TSC], dpT_r[:, j * 128:(j + 1) * 128],
                                       dtT_c[p][:, :], start=True, stop=True).then_inc(sem["sdelm"], 1)
                        nmm = zev[c]
                        done_mm = 0

                        def mm1_group():
                            nonlocal gz
                            g = gz
                            gz += 1
                            th, rem = divmod(g, NFC * NTQ)
                            fc, tq = divmod(rem, NTQ)
                            if tq == 0:
                                eng.wait_ge(sem["sw1q"], 16 * (th * NFC + fc + 1))
                            if fc == 0 and tq == 0:
                                eng.wait_ge(sem["sxnr"], 16 * NDK * (th + 1))
                            if g >= 4:
                                eng.wait_ge(sem["szev"], g - 3)
                            for dk in range(NDK):
                                mm = eng.matmul(ps_mm[g % 4][:, :],
                                                w1q_sb[(th * NFC + fc) % 3][:, dk * 128:(dk + 1) * 128],
                                                xnr[:, dk * (NTQ * TAU) + tq * TAU:dk * (NTQ * TAU) + (tq + 1) * TAU],
                                                start=(dk == 0), stop=(dk == NDK - 1))
                            mm.then_inc(sem["smm1"], 1)

                        for n in range(NN):
                            r = c * 2 * NN + 2 * n
                            if r >= 2:
                                eng.wait_ge(sem["srepe"], r - 1)
                            bank = ps_b6 if n % 2 == 0 else ps_b7
                            eng.matmul(bank[:, 0:TSC], pat_r[:, n * 128:(n + 1) * 128],
                                       Bf_c[p][:, :], start=True, stop=True)
                            eng.matmul(bank[:, TSC:2 * TSC], pat_r[:, n * 128:(n + 1) * 128],
                                       Cf_c[p][:, :], start=True, stop=True).then_inc(sem["srepm"], 2)
                            if want_ffn and n >= 1 and done_mm < min(4, nmm):
                                mm1_group()
                                done_mm += 1
                        if want_ffn:
                            while done_mm < nmm:
                                mm1_group()
                                done_mm += 1
                            if c == 5:
                                eng.wait_ge(sem["sprd"], 2)
                                eng.matmul(ps_mm[0][0:1, 0:1], stat[0][:, 0:1],
                                           ones_sb[:, 0:1], start=True, stop=True)
                                eng.matmul(ps_mm[0][0:1, 1:2], stat[0][:, 1:2],
                                           ones_sb[:, 0:1],
                                           start=True, stop=True).then_inc(sem["smsc"], 1)
                            if c == 6:
                                eng.wait_ge(sem["sprd"], 3)
                                eng.matmul(ps_mm[1][:, 0:1], ones_row[0:1, :],
                                           sclp[0:1, 2:3], start=True, stop=True)
                                eng.matmul(ps_mm[1][:, 1:2], ones_row[0:1, :],
                                           sclp[0:1, 3:4],
                                           start=True, stop=True).then_inc(sem["smsc"], 1)

                if want_ffn:
                    assert gz == NGR
                    eng.wait_ge(sem["sdel"], NCH * NJ)
                    eng.wait_ge(sem["srepe"], NCH * 2 * NN)
                    eng.wait_ge(sem["sdble"], 3 * NCH)
                    eng.wait_ge(sem["szev"], NGR)
                    banks = [ps_mm[0], ps_mm[1], ps_mm[2], ps_mm[3], ps_b4, ps_b5, ps_b6, ps_b7]
                    for tau in range(NTAU):
                        for grp in range(NZG):
                            ri = tau * NZG + grp
                            eng.wait_ge(sem["sgel"], ri + 1)
                            zg = z_sl[ri % ZRING]
                            for fkl in range(FKG):
                                fk = grp * FKG + fkl
                                idx = tau * NFK + fk
                                w2s = [sem["sw2l"], sem["sw2l1"], sem["sw2l2"],
                                       sem["sw2l3"], sem["sw2l4"], sem["sw2l5"],
                                       sem["sw2l6"], sem["sw2l7"]][idx % 8]
                                eng.wait_ge(w2s, 16 * (idx // 8 + 1))
                                if fk == 0:
                                    eng.wait_ge(sem["sz2"], tau * NDM)
                                first = fk == 0
                                last = fk == NFK - 1
                                for dm in range(NDM):
                                    mm = eng.matmul(banks[dm][:, :],
                                                    w2q_sb[idx % 8][:, dm * 128:(dm + 1) * 128],
                                                    zg[:, fkl * TAU:(fkl + 1) * TAU],
                                                    start=first, stop=last)
                                mm.then_inc(sem["smm2"], 1)

            blk.tensor(_tensor_body)

            def _gpsimd_body(eng):
                gq = [0]
                eng.wait_ge(sem["sprm"], 16 * 10)
                eng.memset(hc_b[0][:, 0:1], 0.0)
                if want_ssm:
                    for c in range(NCH):
                        for j in range(NJ):
                            if c * NJ + j >= 1:
                                eng.wait_ge(sem["sstt"], c * NJ + j)
                            for n in range(NN):
                                k = c * 128 + j * NN + n
                                pp = k % 8
                                ph = k % 2
                                eng.wait_ge(sem["sh"], k + 1)
                                eng.wait_ge(sem["srepe"], c * 2 * NN + 2 * (n + 1))
                                if n == 0:
                                    eng.tensor_tensor(y_acc[:, :], h_b[pp][:, :],
                                                      crep[:, n * TSC:(n + 1) * TSC],
                                                      ALU.mult).then_inc(sem["shc"], 1)
                                else:
                                    gq[0] += 1
                                    eng.tensor_tensor(hc_b[ph][:, :], h_b[pp][:, :],
                                                      crep[:, n * TSC:(n + 1) * TSC],
                                                      ALU.mult).then_inc(sem["sgz"], 1)
                                    eng.wait_ge(sem["sgz"], gq[0])
                                    eng.tensor_add(y_acc[:, :], y_acc[:, :],
                                                   hc_b[ph][:, :]).then_inc(sem["shc"], 1)

            blk.gpsimd(_gpsimd_body)

    return nc


def _prep_core_inputs(c, full):
    b, i = c // 2, c % 2
    dlo = i * DH
    perm = np.r_[0:DM] if i == 0 else np.r_[DH:DM, 0:DH]
    half = slice(dlo, dlo + DH)
    x = np.asarray(full["L0_attn_x"][b], dtype=np.float32)
    A_log = np.asarray(full["L0_attn_A_log"], dtype=np.float32)
    Dp = np.asarray(full["L0_attn_D_param"], dtype=np.float32)
    xp = np.asarray(full["L0_attn_x_proj_w"], dtype=np.float32)
    dp = np.asarray(full["L0_attn_dt_proj_w"], dtype=np.float32)
    dpb = np.asarray(full["L0_attn_dt_proj_b"], dtype=np.float32)
    w1 = np.asarray(full["L0_ffn_w1"], dtype=np.float32)
    b1 = np.asarray(full["L0_ffn_b1"], dtype=np.float32)
    w2 = np.asarray(full["L0_ffn_w2"], dtype=np.float32)
    b2 = np.asarray(full["L0_ffn_b2"], dtype=np.float32)

    A_sb = np.ascontiguousarray(
        A_log[half].reshape(NJ, 128, NN).transpose(1, 0, 2).reshape(128, NJ * NN))
    D_col = np.ascontiguousarray(Dp[half].reshape(NJ, 128).T)
    dpb_col = np.ascontiguousarray(dpb[half].reshape(NJ, 128).T)
    dpT = np.ascontiguousarray(dp[half].T)
    xpT = xp.T[perm]
    xpT_sb = np.ascontiguousarray(
        xpT.reshape(NDK, 128, 160).transpose(1, 0, 2).reshape(128, NDK * 160))
    pat = np.zeros((NN, NN * 128), dtype=np.float32)
    for n in range(NN):
        pat[n, n * 128:(n + 1) * 128] = 1.0
    return dict(
        pat_in=pat,
        onesr_in=np.ones((1, 128), dtype=np.float32),
        x_tm=np.ascontiguousarray(x[:, perm]),
        x_dmj=np.ascontiguousarray(
            x.T[half].reshape(NDM, 128, NTAU, TAU).transpose(0, 2, 1, 3)
            .reshape(NDM * NTAU * 128, TAU)),
        A_in=A_sb, D_in=D_col, dpb_in=dpb_col, dpT_in=dpT,
        xpT_in=xpT_sb,
        ones_in=np.ones((128, 1), dtype=np.float32),
        w1t_in=np.ascontiguousarray(
            w1.T[perm].reshape(NDK, 128, 4, 2048).transpose(0, 2, 1, 3)
            .reshape(NFC * 128, 2048)),
        w2T_in=np.ascontiguousarray(w2.T),
        w2Th_in=np.ascontiguousarray(w2.T[:, half]),
        b1_in=np.ascontiguousarray(b1.reshape(NFC, 128).T),
        b2h_in=np.ascontiguousarray(b2[half].reshape(NDM, 128).T),
    )


def kernel(**inputs):
    nc = build()
    in_maps = [_prep_core_inputs(c, inputs) for c in range(8)]
    res = run_bass_kernel_spmd(nc, in_maps, core_ids=list(range(8)))
    out = np.empty((B, L, DM), dtype=np.float32)
    for c in range(8):
        b, i = c // 2, c % 2
        o = res.results[c]["out_dm"].reshape(NDM, NTAU, 128, TAU)
        out[b][:, i * DH:(i + 1) * DH] = o.transpose(0, 2, 1, 3).reshape(DH, L).T
    return out
